# version 23
# baseline (speedup 1.0000x reference)
import sys
from collections import defaultdict

sys.path.insert(0, "/opt/trn_rl_repo")

import numpy as np
import ml_dtypes

import concourse.bass as bass
import concourse.mybir as mybir
from concourse.bass_utils import run_bass_kernel_spmd

F32 = mybir.dt.float32
BF16 = mybir.dt.bfloat16
I32 = mybir.dt.int32
AF = mybir.ActivationFunctionType
ALU = mybir.AluOpType

NCORE = 8
BPC = 4
T = 2048
D = 1024
NCH = 8
TCH = 16
NTILE = 4
HID = 3072
NHC = 6
NPAIR = 8
S127 = float(np.float32(1.0 / 127.0))
LN_EPS = 1e-5
ATT_EPS = 1e-6
INV_D = float(np.float32(1.0) / np.float32(D))

ENGINES = ("sync", "tensor", "scalar", "vector", "gpsimd")


class Sem:
    def __init__(self, handle):
        self.h = handle
        self.n = 0


class Prog:
    def __init__(self):
        self.streams = defaultdict(list)
        self.last_wait = defaultdict(dict)

    def em(self, eng, fn):
        self.streams[eng].append(fn)

    def wait(self, eng, sem, val=None):
        if val is None:
            val = sem.n
        if val <= 0:
            return
        assert val <= sem.n, f"wait on future inc: {val} > {sem.n}"
        lw = self.last_wait[eng]
        if lw.get(sem, 0) >= val:
            return
        lw[sem] = val
        self.em(eng, lambda e, s=sem.h, v=val: e.wait_ge(s, v))

    def op(self, eng, build, inc=None, k=1):
        if inc is not None:
            inc.n += k
            self.em(eng, lambda e, b=build, s=inc.h, kk=k: b(e).then_inc(s, kk))
        else:
            self.em(eng, build)

    def dma(self, eng, sem, out_ap, in_ap):
        self.op(eng, lambda e, o=out_ap, i=in_ap: e.dma_start(out=o, in_=i),
                inc=sem, k=16)

    def flush(self, nc):
        with nc.Block() as blk:
            for eng in ENGINES:
                fns = self.streams.get(eng)
                if not fns:
                    continue
                runner = (lambda fl: lambda e: [f(e) for f in fl])(fns)
                getattr(blk, eng)(runner)


def build_nc(stage=3):
    nc = bass.Bass("TRN2", target_bir_lowering=False,
                   dynamic_dma_scratch_size=8192)

    x_d = nc.dram_tensor("x", [BPC, D, T], F32, kind="ExternalInput")
    wqt_d = nc.dram_tensor("wqt", [D, D], F32, kind="ExternalInput")
    wkt_d = nc.dram_tensor("wkt", [D, D], F32, kind="ExternalInput")
    wvt_d = nc.dram_tensor("wvt", [D, D], F32, kind="ExternalInput")
    wot_d = nc.dram_tensor("wot", [D, D], F32, kind="ExternalInput")
    bq_d = nc.dram_tensor("bq", [128, 8], F32, kind="ExternalInput")
    bo_d = nc.dram_tensor("bo", [128, 8], F32, kind="ExternalInput")
    bk_d = nc.dram_tensor("bk", [1, D], F32, kind="ExternalInput")
    bv_d = nc.dram_tensor("bv", [1, D], F32, kind="ExternalInput")
    wg_d = nc.dram_tensor("wg", [D, HID], I32, kind="ExternalInput")
    wu_d = nc.dram_tensor("wu", [D, HID], I32, kind="ExternalInput")
    wd_d = nc.dram_tensor("wd", [HID, D], I32, kind="ExternalInput")
    ones1_d = nc.dram_tensor("ones1", [1, 128], BF16, kind="ExternalInput")
    ones1f_d = nc.dram_tensor("ones1f", [1, 128], F32, kind="ExternalInput")
    onesc_d = nc.dram_tensor("onesc", [128, 1], BF16, kind="ExternalInput")
    sel_d = nc.dram_tensor("sel", [16, D], BF16, kind="ExternalInput")
    out_d = nc.dram_tensor("out", [BPC, D, T], F32, kind="ExternalOutput")
    stash = {w: nc.dram_tensor(f"stash_{w}", [NHC, 128, 4096], BF16,
                               kind="Internal") for w in ("wg", "wu", "wd")}

    from contextlib import ExitStack
    ctx = ExitStack()
    with ctx:
        SB = lambda name, shape, dt: ctx.enter_context(nc.sbuf_tensor(name, shape, dt))
        SEM = lambda name: Sem(ctx.enter_context(nc.semaphore(name)))

        x32 = SB("x32", [128, NCH * T], F32)
        y_bf = SB("y_bf", [128, NCH * T], BF16)
        arena = SB("arena", [128, 16384], BF16)
        h_arena = SB("h_arena", [128, 2048], BF16)
        wk_bf = SB("wk_bf", [128, 8192], BF16)
        wv_bf = SB("wv_bf", [128, 8192], BF16)
        wq_bf = SB("wq_bf", [128, 8192], BF16)
        wo_bf = SB("wo_bf", [128, 8192], BF16)
        wstage = SB("wstage", [128, 2090], F32)
        wstage_i = SB("wstage_i", [128, 1024], I32)
        bq_sb = SB("bq_sb", [128, 8], F32)
        bo_sb = SB("bo_sb", [128, 8], F32)
        bk_bf = SB("bk_bf", [1, D], BF16)
        bv_bf = SB("bv_bf", [1, D], BF16)
        ones1 = SB("ones1_sb", [1, 128], BF16)
        ones1f = SB("ones1f_sb", [1, 128], F32)
        onesc = SB("onesc_sb", [128, 1], BF16)
        sel = SB("sel_sb", [16, D], BF16)
        zrow = SB("zrow_sb", [1, 128], BF16)
        eps16 = SB("eps16_sb", [16, 1], F32)
        eps1 = SB("eps1_sb", [1, 1], F32)

        kbuf = [arena[:, 0:1024], arena[:, 1024:2048]]
        vbuf = [arena[:, 2048:3088], arena[:, 3088:4128]]
        xc = [arena[:, 4128:5152], arena[:, 5152:6176]]
        xt_bf = arena[:, 0:4096]
        qa_bf = arena[:, 4096:8192]
        e_sb = arena[:, 8192:8704]
        t1_sb = arena[:, 8704:9216]
        zb = [arena[:, 9216:9728], arena[:, 9728:10240]]
        kv_sb = arena[:, 10240:10760]
        den_lhsT = arena[:, 10760:10888]
        z_bf = arena[0:16, 10888:11400]
        e2_sb = arena[:, 12424:12936]
        t2_sb = arena[:, 12936:13448]
        slots = [arena[:, 4096 * i:4096 * (i + 1)] for i in range(4)]
        ln_scr = [(zb[0], zb[1]), (h_arena[:, 0:512], h_arena[:, 512:1024])]
        rb_sb = h_arena[:, 1024:1536]
        mrb_sb = h_arena[:, 1536:2048]
        r16 = arena[0:1, 11400:11912]
        mr16 = arena[0:1, 11912:12424]
        tmp32 = wstage[:, 0:512]
        ln_r = wstage[0:1, 512:1024]
        m_row = wstage[0:1, 1024:1536]
        den32 = wstage[0:16, 1536:2048]
        ln_t = wstage[0:1, 1536:2048]

        ps = [ctx.enter_context(nc.psum_tensor(f"ps{i}", [128, 512], F32))
              for i in range(8)]

        sd = SEM("sd")
        so = SEM("so")
        pe = SEM("pe")
        act = SEM("act")
        dve = SEM("dve")
        pool = SEM("pool")
        stg_sem = [SEM(f"stg{i}") for i in range(2)]
        xq_sem = [SEM(f"xq{i}") for i in range(4)]
        slot_sem = [SEM(f"slot{i}") for i in range(4)]
        st_w = SEM("st_w")
        sqw = SEM("sqw")
        svw = SEM("svw")
        scst = SEM("scst")
        stash_val = {}
        xq_state = {}

        def emit_layernorm(p, t0, src_ready_dve, want_ybf, ps_lo, ps_hi,
                           entry_pe=0, prev_apply=0, scr=None,
                           rows_on_pool=False, entry_rows=None, entry_dve=0):
            scr = scr or ln_scr
            npair = len(scr)
            hist = [entry_pe] * npair
            stat_done = entry_pe
            for m in range(NCH):
                sa, sb_ = scr[m % npair]
                srd = (src_ready_dve[m] if isinstance(src_ready_dve, list)
                       else src_ready_dve)
                p.wait("scalar", dve, srd)
                p.wait("scalar", pe, hist[m % npair])
                p.op("scalar", lambda e, mm=m, t_=t0, d_=sa: e.copy(
                    d_[:, :], x32[:, T * mm + t_:T * mm + t_ + 512]), inc=act)
                cst = act.n
                p.wait("gpsimd", dve, srd)
                p.wait("gpsimd", pe, hist[m % npair])
                p.op("gpsimd", lambda e, mm=m, t_=t0, d_=sb_: e.tensor_mul(
                    d_[:, :], x32[:, T * mm + t_:T * mm + t_ + 512],
                    x32[:, T * mm + t_:T * mm + t_ + 512]), inc=pool)
                sq = pool.n
                if m == 0:
                    p.wait("tensor", dve, entry_dve)
                p.wait("tensor", act, cst)
                p.wait("tensor", pool, sq)
                p.op("tensor", lambda e, mm=m, d_=sa: e.matmul(
                    ps[ps_lo][0:1, 0:512], onesc[:, :], d_[:, :],
                    start=(mm == 0), stop=(mm == NCH - 1)))
                p.op("tensor", lambda e, mm=m, d_=sb_: e.matmul(
                    ps[ps_hi][0:1, 0:512], onesc[:, :], d_[:, :],
                    start=(mm == 0), stop=(mm == NCH - 1)), inc=pe)
                stat_done = pe.n
                hist[m % npair] = pe.n
            p.wait("scalar", pe, stat_done)
            if entry_rows is not None:
                p.wait("scalar", entry_rows[0], entry_rows[1])
            p.op("scalar", lambda e: e.activation(
                m_row[0:1, 0:512], ps[ps_lo][0:1, 0:512], AF.Copy, scale=INV_D))
            p.op("scalar", lambda e: e.activation(
                ln_r[0:1, 0:512], ps[ps_hi][0:1, 0:512], AF.Copy, scale=INV_D),
                inc=act)
            stats_read = act.n
            if rows_on_pool:
                p.wait("gpsimd", act, stats_read)
                p.op("gpsimd", lambda e: e.tensor_mul(
                    ln_t[0:1, 0:512], m_row[0:1, 0:512], m_row[0:1, 0:512]),
                    inc=pool)
                p.wait("gpsimd", pool)
                p.op("gpsimd", lambda e: e.tensor_sub(
                    ln_r[0:1, 0:512], ln_r[0:1, 0:512], ln_t[0:1, 0:512]),
                    inc=pool)
                var_done = pool.n
                p.wait("scalar", pool, var_done)
                p.op("scalar", lambda e: e.activation(
                    ln_r[0:1, 0:512], ln_r[0:1, 0:512], AF.Ln,
                    bias=eps1[0:1, 0:1]), inc=act)
                p.wait("scalar", act)
                p.op("scalar", lambda e: e.activation(
                    ln_t[0:1, 0:512], ln_r[0:1, 0:512], AF.Exp, scale=-0.5),
                    inc=act)
                p.wait("gpsimd", act)
                p.op("gpsimd", lambda e: e.tensor_mul(
                    ln_r[0:1, 0:512], ln_t[0:1, 0:512], m_row[0:1, 0:512]),
                    inc=pool)
                rows_done = (pool, pool.n)
                p.wait("scalar", pool, pool.n)
                p.op("scalar", lambda e: e.copy(r16[0:1, :], ln_t[0:1, 0:512]))
            else:
                p.wait("vector", act, stats_read)
                p.wait("vector", dve)
                p.op("vector", lambda e: e.tensor_mul(
                    tmp32[0:1, 0:512], m_row[0:1, 0:512], m_row[0:1, 0:512]),
                    inc=dve)
                p.wait("vector", dve)
                p.op("vector", lambda e: e.scalar_tensor_tensor(
                    ln_r[0:1, 0:512], ln_r[0:1, 0:512], LN_EPS,
                    tmp32[0:1, 0:512], ALU.add, ALU.subtract), inc=dve)
                p.wait("scalar", dve)
                p.op("scalar", lambda e: e.activation(
                    ln_r[0:1, 0:512], ln_r[0:1, 0:512], AF.Ln), inc=act)
                p.wait("scalar", act)
                p.op("scalar", lambda e: e.activation(
                    tmp32[0:1, 0:512], ln_r[0:1, 0:512], AF.Exp, scale=-0.5),
                    inc=act)
                p.wait("vector", act)
                p.op("vector", lambda e: e.tensor_mul(
                    ln_r[0:1, 0:512], tmp32[0:1, 0:512], m_row[0:1, 0:512]),
                    inc=dve)
                rows_done = (dve, dve.n)
                p.wait("scalar", dve)
                p.op("scalar", lambda e: e.copy(r16[0:1, :], tmp32[0:1, 0:512]))
            p.op("scalar", lambda e: e.copy(mr16[0:1, :], ln_r[0:1, 0:512]),
                 inc=act)
            p.wait("tensor", act)
            p.op("tensor", lambda e: e.matmul(
                ps[ps_lo][:, 0:512], ones1[:, :], r16[0:1, :],
                start=True, stop=True))
            p.op("tensor", lambda e: e.matmul(
                ps[ps_hi][:, 0:512], ones1[:, :], mr16[0:1, :],
                start=True, stop=True), inc=pe)
            bc = pe.n
            p.wait("scalar", pe, bc)
            p.wait("scalar", dve, prev_apply)
            p.op("scalar", lambda e: e.copy(rb_sb[:, :], ps[ps_lo][:, 0:512]))
            p.op("scalar", lambda e: e.copy(mrb_sb[:, :], ps[ps_hi][:, 0:512]),
                 inc=act)
            bcast_evac = act.n
            p.wait("vector", act, bcast_evac)
            for m in range(NCH):
                p.wait("vector", dve)
                p.op("vector", lambda e, mm=m, t_=t0: e.tensor_mul(
                    tmp32[:, :], x32[:, T * mm + t_:T * mm + t_ + 512],
                    rb_sb[:, :]), inc=dve)
                p.wait("vector", dve)
                p.op("vector", lambda e, mm=m, t_=t0: e.tensor_sub(
                    x32[:, T * mm + t_:T * mm + t_ + 512], tmp32[:, :],
                    mrb_sb[:, :]), inc=dve)
                if want_ybf:
                    p.wait("scalar", dve)
                    p.op("scalar", lambda e, mm=m, t_=t0: e.copy(
                        y_bf[:, T * mm + t_:T * mm + t_ + 512],
                        x32[:, T * mm + t_:T * mm + t_ + 512]), inc=act)
            return stats_read, dve.n, bcast_evac, rows_done

        p = Prog()
        p.dma("sync", scst, bq_sb[:], bq_d[:])
        p.dma("sync", scst, bo_sb[:], bo_d[:])
        p.dma("sync", scst, ones1[:], ones1_d[:])
        p.dma("sync", scst, ones1f[:], ones1f_d[:])
        p.dma("sync", scst, onesc[:], onesc_d[:])
        p.dma("sync", scst, sel[:], sel_d[:])
        p.dma("sync", sd, wstage[0:1, 0:D], bk_d[:])
        for _eng in ("tensor", "scalar", "vector"):
            p.wait(_eng, scst)
        p.wait("scalar", sd)
        p.op("scalar", lambda e: e.copy(bk_bf[:], wstage[0:1, 0:D]), inc=act)
        p.wait("sync", act)
        p.dma("sync", sd, wstage[0:1, 0:D], bv_d[:])
        p.wait("scalar", sd)
        p.op("scalar", lambda e: e.copy(bv_bf[:], wstage[0:1, 0:D]), inc=act)
        p.op("vector", lambda e: e.memset(den_lhsT[:], 0.0))
        p.op("vector", lambda e: e.memset(eps16[:], ATT_EPS))
        p.op("vector", lambda e: e.memset(eps1[:], LN_EPS))
        p.op("vector", lambda e: e.memset(zrow[:], 0.0), inc=dve)
        init_dve = dve.n
        wcast = {}

        def prefetch_x(b):
            st = []
            for quar in range(4):
                qs = xq_sem[quar]
                p.dma("sync", qs,
                      x32[:].rearrange("p (k t) -> p k t", k=NCH)
                      [:, :, 512 * quar:512 * (quar + 1)],
                      x_d[b].rearrange("(k p) t -> p k t", p=128)
                      [:, :, 512 * quar:512 * (quar + 1)])
                st.append((qs, qs.n))
            xq_state[b] = st

        bias_ready = act.n
        prefetch_x(0)
        wcast["wk"] = []
        wcast["wv"] = []
        p.wait("scalar", act, bias_ready)
        p.dma("scalar", sqw, wstage[:, 0:D], wkt_d[0:128, :])
        p.dma("scalar", svw, wstage[:, 1045:1045 + D], wvt_d[0:128, :])
        for k in range(NCH):
            p.wait("scalar", sqw, 16 * (k + 1))
            p.op("scalar", lambda e, kk=k:
                 e.copy(wk_bf[:, D * kk:D * (kk + 1)], wstage[:, 0:D]),
                 inc=act)
            wcast["wk"].append(act.n)
            if k + 1 < NCH:
                p.wait("scalar", act, wcast["wk"][k])
                p.dma("scalar", sqw, wstage[:, 0:D],
                      wkt_d[128 * (k + 1):128 * (k + 2), :])
            p.wait("scalar", svw, 16 * (k + 1))
            p.op("scalar", lambda e, kk=k:
                 e.copy(wv_bf[:, D * kk:D * (kk + 1)],
                        wstage[:, 1045:1045 + D]), inc=act)
            wcast["wv"].append(act.n)
            if k + 1 < NCH:
                p.wait("scalar", act, wcast["wv"][k])
                p.dma("scalar", svw, wstage[:, 1045:1045 + D],
                      wvt_d[128 * (k + 1):128 * (k + 2), :])
        kv_w_ready = act.n

        wcast["wq"] = []
        wcast["wo"] = []
        wq_dma = []
        wo_dma = []
        p.wait("sync", act, kv_w_ready)
        p.dma("sync", sd, wstage[:, 1045:1045 + D], wqt_d[0:128, :])
        wq_dma.append(sd.n)
        p.dma("sync", sqw, wstage[:, 0:D], wot_d[0:128, :])
        wo_dma.append(sqw.n)
        defer_cast = {2: [("wq", 0)], 3: [("wq", 1)], 4: [("wq", 2)],
                      5: [("wq", 3)], 6: [("wq", 4)], 7: [("wq", 5)],
                      8: [("wq", 6)], 9: [("wq", 7)],
                      10: [("wo", 0), ("wo", 1)], 11: [("wo", 2), ("wo", 3)],
                      12: [("wo", 4)], 13: [("wo", 5)], 14: [("wo", 6)],
                      15: [("wo", 7)]}

        def emit_deferred_cast(wname, j):
            qsem, slot, wsb, wdram, dvals = (
                (sd, 1045, wq_bf, wqt_d, wq_dma) if wname == "wq"
                else (sqw, 0, wo_bf, wot_d, wo_dma))
            p.wait("scalar", qsem, dvals[j])
            p.op("scalar", lambda e, jj=j, w=wsb, so_=slot:
                 e.copy(w[:, D * jj:D * (jj + 1)], wstage[:, so_:so_ + D]),
                 inc=act)
            wcast[wname].append(act.n)
            if j + 1 < NCH:
                p.wait("sync", act, wcast[wname][j])
                p.dma("sync", qsem, wstage[:, slot:slot + D],
                      wdram[128 * (j + 1):128 * (j + 2), :])
                dvals.append(qsem.n)

        prev_rows = None
        ln2_rows_pool = 0
        for b in range(BPC):
            xq_loaded = xq_state[b]
            if b > 0:
                p.wait("tensor", act, prev_guard["p3_act"])
                p.wait("tensor", dve, prev_guard["ffn_done"])
                p.wait("tensor", act, prev_guard["ln2_evac"])
                p.wait("vector", pe)
                for _ss in slot_sem:
                    p.wait("scalar", _ss)

            kevac_done = [0] * TCH
            phi_free = [0, 0]
            vevac_done = [0] * TCH
            kv_done = [0] * TCH
            xc_done = [0] * TCH
            xcw = [0] * TCH
            psK_free = [0, 0]
            psV_free = [0, 0]

            def emit_xc(c):
                qs, qv = xq_loaded[c // 4]
                p.wait("vector", qs, qv)
                if c >= 2:
                    p.wait("vector", pe, xcw[c - 2])
                p.op("vector", lambda e, cc=c: e.tensor_copy(
                    xc[cc % 2][:].rearrange("p (k t) -> p k t", k=NCH),
                    x32[:].rearrange("p (k t) -> p k t", k=NCH)
                    [:, :, 128 * cc:128 * (cc + 1)]), inc=dve)
                xc_done[c] = dve.n

            def emit_vgroup(c, h):
                p.wait("tensor", act, max(psV_free[h], bias_ready))
                p.wait("tensor", dve, xc_done[c])
                for k in range(NCH):
                    p.wait("tensor", act, wcast["wv"][k])
                    p.op("tensor", lambda e, cc=c, hh=h, kk=k:
                         e.matmul(ps[2 + hh][:, 0:512],
                                  xc[cc % 2][:, 128 * kk:128 * (kk + 1)],
                                  wv_bf[:, D * kk + 512 * hh:D * kk + 512 * (hh + 1)],
                                  start=(kk == 0), stop=False))
                p.op("tensor", lambda e, hh=h:
                     e.matmul(ps[2 + hh][:, 0:512], ones1[:, :],
                              bv_bf[0:1, 512 * hh:512 * (hh + 1)],
                              start=False, stop=True), inc=pe)
                return pe.n

            def emit_vevac(c, h, vproj_h):
                p.wait("scalar", pe, vproj_h)
                p.wait("scalar", dve, vmemset_done)
                if c >= 2:
                    p.wait("scalar", pe, kv_done[c - 2])
                voff = 2048 + 1040 * (c % 2) + 520 * h
                p.op("scalar", lambda e, hh=h, vo=voff:
                     e.copy(bass.AP(arena, vo, [[16384, 128], [65, 8], [1, 64]]),
                            ps[2 + hh][:, 0:512].rearrange(
                                "p (i s) -> p i s", s=64)), inc=act)
                psV_free[h] = act.n

            def emit_kv(c):
                p.wait("tensor", dve, kevac_done[c])
                p.wait("tensor", act, vevac_done[c])
                for pr in range(NPAIR):
                    bank = 4 + pr // 2
                    col0 = 130 * (pr % 2)
                    p.op("tensor", lambda e, cc=c, b_=bank, c0=col0, pp=pr:
                         e.matmul(ps[b_][:, c0:c0 + 130],
                                  kbuf[cc % 2][:, 128 * pp:128 * (pp + 1)],
                                  vbuf[cc % 2][:, 130 * pp:130 * (pp + 1)],
                                  start=False,
                                  stop=(cc == TCH - 1),
                                  skip_group_check=True),
                         inc=(pe if pr == NPAIR - 1 else None))
                kv_done[c] = pe.n

            def emit_proj_and_evacs(c, prev):
                kproj_h = [0, 0]
                for h in range(2):
                    p.wait("tensor", act, max(psK_free[h], bias_ready))
                    p.wait("tensor", dve, xc_done[c])
                    for k in range(NCH):
                        p.wait("tensor", act, wcast["wk"][k])
                        p.op("tensor", lambda e, cc=c, hh=h, kk=k:
                             e.matmul(ps[hh][:, 0:512],
                                      xc[cc % 2][:, 128 * kk:128 * (kk + 1)],
                                      wk_bf[:, D * kk + 512 * hh:D * kk + 512 * (hh + 1)],
                                      start=(kk == 0), stop=False))
                    p.op("tensor", lambda e, hh=h:
                         e.matmul(ps[hh][:, 0:512], ones1[:, :],
                                  bk_bf[0:1, 512 * hh:512 * (hh + 1)],
                                  start=False, stop=True), inc=pe)
                    kproj_h[h] = pe.n
                kproj = pe.n
                v0 = emit_vgroup(c, 0)
                emit_vevac(c, 0, v0)
                if prev is not None:
                    emit_kv(prev)
                v1 = emit_vgroup(c, 1)
                xcw[c] = pe.n
                emit_vevac(c, 1, v1)
                vevac_done[c] = act.n
                for h in range(2):
                    eb, tb = (e_sb, t1_sb) if h == 0 else (e2_sb, t2_sb)
                    p.wait("scalar", pe, kproj_h[h])
                    p.wait("scalar", dve, phi_free[h])
                    p.op("scalar", lambda e, hh=h, b1=eb:
                         e.activation(b1[:, :], ps[hh][:, 0:512], AF.Exp))
                    p.op("scalar", lambda e, hh=h, b2=tb:
                         e.activation(b2[:, :], ps[hh][:, 0:512], AF.Relu),
                         inc=act)
                    psK_free[h] = act.n
                    p.wait("vector", act)
                    if c >= 2:
                        p.wait("vector", pe, kv_done[c - 2])
                    p.op("vector", lambda e, cc=c, hh=h, b1=eb, b2=tb:
                         e.scalar_tensor_tensor(
                             kbuf[cc % 2][:, 512 * hh:512 * (hh + 1)],
                             b1[:, :], 1.0, b2[:, :], ALU.min, ALU.add),
                         inc=dve)
                    phi_free[h] = dve.n
                kevac_done[c] = dve.n

            for vb in range(2):
                base = 2048 + 1040 * vb + 64
                p.op("vector", lambda e, b_=base: e.memset(
                    bass.AP(arena, b_, [[16384, 128], [65, 16], [1, 1]]), 1.0),
                    inc=(dve if vb == 1 else None))
            vmemset_done = dve.n

            p.wait("tensor", act, bias_ready)
            p.wait("tensor", dve, init_dve)
            for t_ in range(4):
                p.op("tensor", lambda e, b_=4 + t_:
                     e.matmul(ps[b_][:, 0:260], zrow[:, :], bv_bf[0:1, 0:260],
                              start=True, stop=False, skip_group_check=True))

            emit_xc(0)
            prev = None
            for c in range(TCH):
                if c + 1 < TCH:
                    emit_xc(c + 1)
                emit_proj_and_evacs(c, prev)
                if b == 0:
                    for wname_, j_ in defer_cast.get(c, []):
                        emit_deferred_cast(wname_, j_)
                prev = c
            emit_kv(TCH - 1)

            p.wait("scalar", pe, kv_done[TCH - 1])
            for pr in range(NPAIR):
                bank = 4 + pr // 2
                c0 = 130 * (pr % 2)
                p.op("scalar", lambda e, b_=bank, cc=c0, pp=pr:
                     e.copy(kv_sb[0:64, 65 * pp:65 * (pp + 1)],
                            ps[b_][0:64, cc:cc + 65]))
                p.op("scalar", lambda e, b_=bank, cc=c0, pp=pr:
                     e.copy(kv_sb[64:128, 65 * pp:65 * (pp + 1)],
                            ps[b_][64:128, cc + 65:cc + 130]), inc=act)
            p.wait("scalar", act)
            for hh in range(16):
                r0 = 64 * (hh % 2)
                col = 16 * (hh // 2) + hh
                p.op("scalar", lambda e, r=r0, cl=col, pp=hh // 2:
                     e.activation(den_lhsT[r:r + 64, cl:cl + 1],
                                  kv_sb[r:r + 64, 65 * pp + 64:65 * pp + 65],
                                  AF.Copy, bias=ATT_EPS), inc=act)
            kv_ready = act.n

            q_free = [psK_free[0], psK_free[1], psV_free[0], psV_free[1]]
            q_free_d = [0, 0, 0, 0]
            zb_free = [0, 0]
            q_stt = [0] * NCH
            a_mult = [0] * NCH
            stats_free = kv_ready
            bcast_free = 0
            ln1_apply = 0
            q_mm_last = kv_done[TCH - 1]
            xt_ready_t = [0] * (NTILE + 1)

            def emit_xt(tile):
                t0_ = 512 * tile
                p.wait("vector", pe, q_mm_last)
                qs_, qv_ = xq_loaded[tile]
                p.wait("vector", qs_, qv_)
                p.op("vector", lambda e, t_=t0_: e.tensor_copy(
                    xt_bf[:].rearrange("p (k t) -> p k t", k=NCH),
                    x32[:].rearrange("p (k t) -> p k t", k=NCH)
                    [:, :, t_:t_ + 512]), inc=dve)
                xt_ready_t[tile] = dve.n

            emit_xt(0)
            for tile in range(NTILE):
                t0 = 512 * tile
                xt_ready = xt_ready_t[tile]
                for m in range(NCH):
                    bank = m % 4
                    p.wait("tensor", act, q_free[bank])
                    p.wait("tensor", dve, max(q_free_d[bank], xt_ready))
                    for k in range(NCH):
                        p.wait("tensor", act, wcast["wq"][k])
                        p.op("tensor", lambda e, mm=m, kk=k, b_=bank:
                             e.matmul(ps[b_][:, 0:512],
                                      wq_bf[:, D * kk + 128 * mm:D * kk + 128 * (mm + 1)],
                                      xt_bf[:, 512 * kk:512 * (kk + 1)],
                                      start=(kk == 0), stop=(kk == NCH - 1)),
                             inc=(pe if k == NCH - 1 else None))
                    qmm = pe.n
                    eb, tb = (e_sb, t1_sb) if m % 2 == 0 else (e2_sb, t2_sb)
                    p.wait("scalar", pe, qmm)
                    p.wait("scalar", dve, phi_free[m % 2])
                    p.op("scalar", lambda e, b_=bank, mm=m, b1=eb:
                         e.activation(b1[:, :], ps[b_][:, 0:512], AF.Exp,
                                      bias=bq_sb[:, mm:mm + 1]))
                    p.op("scalar", lambda e, b_=bank, mm=m, b2=tb:
                         e.activation(b2[:, :], ps[b_][:, 0:512], AF.Relu,
                                      bias=bq_sb[:, mm:mm + 1]), inc=act)
                    q_free[bank] = act.n
                    p.wait("vector", act)
                    p.op("vector", lambda e, mm=m, b1=eb, b2=tb:
                         e.scalar_tensor_tensor(qa_bf[:, 512 * mm:512 * (mm + 1)],
                                                b1[:, :], 1.0, b2[:, :],
                                                ALU.min, ALU.add), inc=dve)
                    q_stt[m] = dve.n
                    phi_free[m % 2] = dve.n
                q_all = dve.n
                q_mm_last = pe.n
                if tile + 1 < NTILE:
                    emit_xt(tile + 1)
                p.wait("tensor", dve, q_free_d[2])
                p.wait("tensor", act, kv_ready)
                p.wait("tensor", act, q_free[2])
                for c in range(NCH):
                    p.wait("tensor", dve, q_stt[c])
                    p.op("tensor", lambda e, cc=c:
                         e.matmul(ps[2][0:16, 0:512],
                                  den_lhsT[:, 16 * cc:16 * (cc + 1)],
                                  qa_bf[:, 512 * cc:512 * (cc + 1)],
                                  start=(cc == 0), stop=(cc == NCH - 1)),
                         inc=(pe if c == NCH - 1 else None))
                p.wait("scalar", pe, pe.n)
                p.wait("scalar", pool, ln2_rows_pool)
                p.op("scalar", lambda e: e.activation(
                    den32[:, :], ps[2][0:16, 0:512], AF.Ln,
                    bias=eps16[:, 0:1]), inc=act)
                q_free[2] = act.n
                p.wait("scalar", act)
                p.op("scalar", lambda e: e.activation(
                    z_bf[:, :], den32[:, :], AF.Exp, scale=-1.0), inc=act)
                z_ready = act.n
                for c in range(NCH):
                    p.wait("tensor", act, z_ready)
                    p.wait("tensor", act)
                    p.op("tensor", lambda e, cc=c:
                         e.matmul(ps[3][:, 0:512], sel[:, 128 * cc:128 * (cc + 1)],
                                  z_bf[:, :], start=True, stop=True), inc=pe)
                    zmm = pe.n
                    p.wait("scalar", pe, zmm)
                    p.wait("scalar", dve, zb_free[c % 2])
                    p.op("scalar", lambda e, cc=c:
                         e.copy(zb[cc % 2][:, :], ps[3][:, 0:512]), inc=act)
                    zev = act.n
                    bank = 4 + c % 2
                    p.wait("tensor", dve, zb_free[c % 2])
                    p.op("tensor", lambda e, cc=c, b_=bank:
                         e.matmul(ps[b_][0:64, 0:512],
                                  kv_sb[0:64, 65 * cc:65 * cc + 64],
                                  qa_bf[0:64, 512 * cc:512 * (cc + 1)],
                                  start=True, stop=True))
                    p.op("tensor", lambda e, cc=c, b_=bank:
                         e.matmul(ps[b_][64:128, 0:512],
                                  kv_sb[64:128, 65 * cc:65 * cc + 64],
                                  qa_bf[64:128, 512 * cc:512 * (cc + 1)],
                                  start=True, stop=True), inc=pe)
                    amm = pe.n
                    p.wait("vector", pe, amm)
                    p.wait("vector", act, zev)
                    p.op("vector", lambda e, cc=c, b_=bank:
                         e.tensor_mul(qa_bf[:, 512 * cc:512 * (cc + 1)],
                                      ps[b_][:, 0:512], zb[cc % 2][:, :]),
                         inc=dve)
                    zb_free[c % 2] = dve.n
                    a_mult[c] = dve.n
                a_all = dve.n
                q_free[3] = act.n
                o_free = [0, 0]
                for m in range(NCH):
                    bank = 6 + m % 2
                    p.wait("tensor", dve, o_free[m % 2])
                    p.wait("tensor", act, max(stats_free, bcast_free))
                    for k in range(NCH):
                        p.wait("tensor", act, wcast["wo"][k])
                        p.wait("tensor", dve, a_mult[k])
                        p.op("tensor", lambda e, mm=m, kk=k, b_=bank:
                             e.matmul(ps[b_][:, 0:512],
                                      wo_bf[:, D * kk + 128 * mm:D * kk + 128 * (mm + 1)],
                                      qa_bf[:, 512 * kk:512 * (kk + 1)],
                                      start=(kk == 0), stop=(kk == NCH - 1)),
                             inc=(pe if k == NCH - 1 else None))
                    omm = pe.n
                    p.wait("vector", pe, omm)
                    p.op("vector", lambda e, mm=m, b_=bank, t_=t0:
                         e.scalar_tensor_tensor(
                             x32[:, T * mm + t_:T * mm + t_ + 512],
                             ps[b_][:, 0:512], bo_sb[:, mm:mm + 1],
                             x32[:, T * mm + t_:T * mm + t_ + 512],
                             ALU.add, ALU.add), inc=dve)
                    o_free[m % 2] = dve.n
                o_done = dve.n
                stats_free, ln1_apply, bcast_free, prev_rows = emit_layernorm(
                    p, t0, o_done, True, 6, 7, prev_apply=ln1_apply,
                    entry_rows=prev_rows)

            if stage == 1:
                p.flush(nc)
                p = Prog()
                p.wait("sync", dve)
                p.wait("sync", act)
                p.dma("sync", so,
                      out_d[b].rearrange("(k p) t -> p k t", p=128),
                      x32[:].rearrange("p (k t) -> p k t", k=NCH))
                p.wait("sync", so)
                p.flush(nc)
                continue

            p3_pe = pe.n
            p3_act = act.n
            p3_dve = dve.n
            for _eng in ("gpsimd", "vector", "scalar"):
                p.wait(_eng, pe, p3_pe)
                p.wait(_eng, act, p3_act)
                p.wait(_eng, dve, p3_dve)
            stg_free = [None, None]
            stg_of = [0, 512]
            conv_eng = {"wg": ("gpsimd", pool), "wg2": ("vector", dve),
                        "wu": ("vector", dve), "wd": ("scalar", act)}
            load_q = {"wg": "sync", "wu": "sync", "wd": "scalar"}
            slot_map = [(0, 1, 2), (0, 1, 3), (0, 1, 2), (0, 1, 3),
                        (0, 1, 2), (0, 1, 3)]
            slot_war = [p3_pe] * 4
            slot_ready_k = [dict() for _ in range(4)]
            stash_rd = {}
            slot_stash_rd = {}

            def emit_ffn_loads(hc):
                sgl, sul, sdl = slot_map[hc]
                for wname, wdram, slot_i in (("wg", wg_d, sgl),
                                             ("wu", wu_d, sul),
                                             ("wd", wd_d, sdl)):
                    dest = slots[slot_i]
                    if b > 0:
                        ssem = slot_sem[slot_i]
                        ldq = "scalar" if wname == "wu" else "sync"
                        p.wait(ldq, *stash_val[(wname, hc)])
                        p.wait(ldq, pe, slot_war[slot_i])
                        p.dma(ldq, ssem, dest[:, :], stash[wname][hc])
                        rdy = (ssem, ssem.n)
                        for kk in range(NCH):
                            slot_ready_k[slot_i][kk] = rdy
                        continue
                    qeng = load_q[wname]
                    for s in range(8):
                        ceng, csem = conv_eng[
                            "wg2" if (wname == "wg" and s % 2) else wname]
                        sub = 0 if wname != "wd" else 1
                        qsem = stg_sem[sub]
                        col = stg_of[sub]
                        if stg_free[sub] is not None:
                            p.wait(qeng, *stg_free[sub])
                        if wname in ("wg", "wu"):
                            srcw = wdram[128 * s:128 * (s + 1),
                                         512 * hc:512 * (hc + 1)]
                            dcol = 512 * s
                            kk = s
                        else:
                            r0 = 512 * hc + 128 * (s // 2)
                            c0 = 512 * (s % 2)
                            srcw = wdram[r0:r0 + 128, c0:c0 + 512]
                            dcol = 1024 * (s // 2) + 512 * (s % 2)
                            kk = s // 2
                        p.op(qeng, lambda e, cl=col, sr=srcw:
                             e.dma_start(out=wstage_i[:, cl:cl + 512], in_=sr),
                             inc=qsem, k=16)
                        p.wait(ceng, qsem, qsem.n)
                        p.wait(ceng, pe, slot_war[slot_i])
                        if (wname, hc) in stash_rd:
                            p.wait(ceng, *stash_rd.pop((wname, hc), (st_w, 0)))
                        prev_rd = slot_stash_rd.get(slot_i)
                        if prev_rd is not None:
                            p.wait(ceng, *prev_rd)
                        if ceng == "scalar":
                            p.op(ceng, lambda e, cl=col, dc=dcol, d_=dest:
                                 e.copy(d_[:, dc:dc + 512],
                                        wstage_i[:, cl:cl + 512]), inc=csem)
                        else:
                            p.op(ceng, lambda e, cl=col, dc=dcol, d_=dest:
                                 e.tensor_copy(d_[:, dc:dc + 512],
                                               wstage_i[:, cl:cl + 512]),
                                 inc=csem)
                        stg_free[sub] = (csem, csem.n)
                        slot_ready_k[slot_i][kk] = (csem, csem.n)
                        last_conv = csem.n
                    ssem = slot_sem[slot_i]
                    p.wait("sync", csem, last_conv)
                    p.dma("sync", ssem, stash[wname][hc], dest[:, :])
                    stash_val[(wname, hc)] = (ssem, ssem.n)
                    slot_stash_rd[slot_i] = (ssem, ssem.n)

            emit_ffn_loads(0)
            gate_free = [p3_act, p3_act]
            up_free = [p3_dve, p3_dve]
            down_free = [p3_dve] * 4
            h_war = 0
            x32_guard = [[p3_dve] * NCH for _ in range(NTILE)]

            for hc in range(NHC):
                sgl, sul, sdl = slot_map[hc]
                for tile in range(NTILE):
                    t0 = 512 * tile
                    silu_done = [0] * 4
                    h_stt = [0] * 4
                    for g in range(4):
                        bank = g % 2
                        p.wait("tensor", act, gate_free[bank])
                        for k in range(NCH):
                            p.wait("tensor", *slot_ready_k[sgl][k])
                            p.op("tensor", lambda e, gg=g, kk=k, b_=bank, sl_=sgl, t_=t0:
                                 e.matmul(ps[b_][:, 0:512],
                                          slots[sl_][:, 512 * kk + 128 * gg:512 * kk + 128 * (gg + 1)],
                                          y_bf[:, T * kk + t_:T * kk + t_ + 512],
                                          start=(kk == 0), stop=(kk == NCH - 1)),
                                 inc=(pe if k == NCH - 1 else None))
                        gmm = pe.n
                        slot_war[sgl] = pe.n
                        p.wait("scalar", pe, gmm)
                        p.wait("scalar", pe, h_war)
                        p.op("scalar", lambda e, gg=g, b_=bank:
                             e.activation(h_arena[:, 512 * gg:512 * (gg + 1)],
                                          ps[b_][:, 0:512], AF.Silu,
                                          scale=S127), inc=act)
                        silu_done[g] = act.n
                        gate_free[bank] = act.n
                    for g in range(4):
                        bank = 2 + g % 2
                        p.wait("tensor", dve, up_free[g % 2])
                        for k in range(NCH):
                            p.wait("tensor", *slot_ready_k[sul][k])
                            p.op("tensor", lambda e, gg=g, kk=k, b_=bank, sl_=sul, t_=t0:
                                 e.matmul(ps[b_][:, 0:512],
                                          slots[sl_][:, 512 * kk + 128 * gg:512 * kk + 128 * (gg + 1)],
                                          y_bf[:, T * kk + t_:T * kk + t_ + 512],
                                          start=(kk == 0), stop=(kk == NCH - 1)),
                                 inc=(pe if k == NCH - 1 else None))
                        umm = pe.n
                        slot_war[sul] = pe.n
                        p.wait("vector", pe, umm)
                        p.wait("vector", act, silu_done[g])
                        p.wait("vector", pe, h_war)
                        p.op("vector", lambda e, gg=g, b_=bank:
                             e.scalar_tensor_tensor(
                                 h_arena[:, 512 * gg:512 * (gg + 1)],
                                 ps[b_][:, 0:512], S127,
                                 h_arena[:, 512 * gg:512 * (gg + 1)],
                                 ALU.mult, ALU.mult), inc=dve)
                        up_free[g % 2] = dve.n
                        h_stt[g] = dve.n
                    for m in range(NCH):
                        bank = 4 + m % 4
                        p.wait("tensor", dve, down_free[m % 4])
                        for s in range(4):
                            p.wait("tensor", *slot_ready_k[sdl][s])
                            p.wait("tensor", dve, h_stt[s])
                            p.op("tensor", lambda e, mm=m, ss=s, b_=bank, sl_=sdl:
                                 e.matmul(ps[b_][:, 0:512],
                                          slots[sl_][:, 1024 * ss + 128 * mm:1024 * ss + 128 * (mm + 1)],
                                          h_arena[:, 512 * ss:512 * (ss + 1)],
                                          start=(ss == 0), stop=(ss == 3)),
                                 inc=(pe if s == 3 else None))
                        dmm = pe.n
                        if m == NCH - 1:
                            h_war = pe.n
                        slot_war[sdl] = pe.n
                        p.wait("vector", pe, dmm)
                        p.wait("vector", dve, x32_guard[tile][m])
                        p.op("vector", lambda e, mm=m, b_=bank, t_=t0:
                             e.scalar_tensor_tensor(
                                 x32[:, T * mm + t_:T * mm + t_ + 512],
                                 ps[b_][:, 0:512], S127,
                                 x32[:, T * mm + t_:T * mm + t_ + 512],
                                 ALU.mult, ALU.add), inc=dve)
                        down_free[m % 4] = dve.n
                        x32_guard[tile][m] = dve.n
                if hc + 1 < NHC:
                    emit_ffn_loads(hc + 1)
            ffn_done = dve.n
            prev_guard = {"p3_act": act.n, "ffn_done": ffn_done, "ln2_evac": 0}

            for _ss in slot_sem:
                p.wait("scalar", _ss)
                p.wait("gpsimd", _ss)
            p.wait("scalar", pe)
            p.wait("gpsimd", pe)
            prev_apply = ffn_done
            prev_bc = pe.n
            ln2_apply = 0
            ln2_scr = [(zb[0], zb[1]),
                       (h_arena[:, 0:512], h_arena[:, 512:1024]),
                       (slots[0][:, 0:512], slots[0][:, 512:1024]),
                       (slots[1][:, 0:512], slots[1][:, 512:1024])]
            next_x = []
            for tile in range(NTILE):
                t0 = 512 * tile
                _, ap_done, _bev, prev_rows = emit_layernorm(
                    p, t0, x32_guard[tile], False, 6, 7, entry_pe=prev_bc,
                    prev_apply=ln2_apply, scr=ln2_scr,
                    rows_on_pool=True, entry_rows=prev_rows,
                    entry_dve=(ffn_done if tile == 0 else 0))
                ln2_rows_pool = prev_rows[1]
                ln2_apply = ap_done
                prev_bc = pe.n
                prev_apply = ap_done
                prev_guard["ln2_evac"] = _bev
                p.wait("sync", dve, ap_done)
                p.dma("sync", so,
                      out_d[b].rearrange("(k p) t -> p k t", p=128)
                      [:, :, t0:t0 + 512],
                      x32[:].rearrange("p (k t) -> p k t", k=NCH)
                      [:, :, t0:t0 + 512])
                if b + 1 < BPC:
                    p.wait("sync", so, so.n)
                    qs = xq_sem[tile]
                    p.dma("sync", qs,
                          x32[:].rearrange("p (k t) -> p k t", k=NCH)
                          [:, :, t0:t0 + 512],
                          x_d[b + 1].rearrange("(k p) t -> p k t", p=128)
                          [:, :, t0:t0 + 512])
                    next_x.append((qs, qs.n))
            if b + 1 < BPC:
                xq_state[b + 1] = next_x
            p.wait("sync", so)
        p.flush(nc)

    return nc


def _host_inputs(L0_attn_x, L0_attn_Wq, L0_attn_bq, L0_attn_Wk, L0_attn_bk,
                 L0_attn_Wv, L0_attn_bv, L0_attn_Wo, L0_attn_bo,
                 L0_ffn_w_gate, L0_ffn_w_up, L0_ffn_w_down):
    bf = ml_dtypes.bfloat16
    x = np.asarray(L0_attn_x, dtype=np.float32)
    x_fm = np.ascontiguousarray(
        x.reshape(NCORE, BPC, T, D).transpose(0, 1, 3, 2))

    wqt = np.ascontiguousarray(np.asarray(L0_attn_Wq, np.float32).T)
    wkt = np.ascontiguousarray(np.asarray(L0_attn_Wk, np.float32).T)
    wot = np.ascontiguousarray(np.asarray(L0_attn_Wo, np.float32).T)
    wvt = np.ascontiguousarray(np.asarray(L0_attn_Wv, np.float32).T)
    bv_pad = np.ascontiguousarray(np.asarray(L0_attn_bv, np.float32)[None, :])

    bq_pc = np.ascontiguousarray(np.asarray(L0_attn_bq, np.float32).reshape(8, 128).T)
    bo_pc = np.ascontiguousarray(np.asarray(L0_attn_bo, np.float32).reshape(8, 128).T)
    bk_row = np.ascontiguousarray(np.asarray(L0_attn_bk, np.float32)[None, :])

    wg = np.ascontiguousarray(np.asarray(L0_ffn_w_gate, np.int32).T)
    wu = np.ascontiguousarray(np.asarray(L0_ffn_w_up, np.int32).T)
    wd = np.ascontiguousarray(np.asarray(L0_ffn_w_down, np.int32).T)

    ones1 = np.ones((1, 128), bf)
    ones1f = np.ones((1, 128), np.float32)
    onesc = np.ones((128, 1), bf)
    sel = np.zeros((16, D), bf)
    for c in range(NCH):
        sel[2 * c, 128 * c:128 * c + 64] = 1
        sel[2 * c + 1, 128 * c + 64:128 * (c + 1)] = 1

    shared = dict(wqt=wqt, wkt=wkt, wvt=wvt, wot=wot, bq=bq_pc, bo=bo_pc,
                  bk=bk_row, bv=bv_pad, wg=wg, wu=wu, wd=wd, ones1=ones1,
                  ones1f=ones1f, onesc=onesc, sel=sel)
    return [dict(shared, x=np.ascontiguousarray(x_fm[c])) for c in range(NCORE)]


def kernel(**inputs):
    in_maps = _host_inputs(**inputs)
    nc = build_nc(stage=3)
    res = run_bass_kernel_spmd(nc, in_maps, core_ids=list(range(NCORE)))
    out_fm = np.stack([np.asarray(r["out"]) for r in res.results])
    out = out_fm.transpose(0, 1, 3, 2).reshape(NCORE * BPC, T, D)
    return np.ascontiguousarray(out).astype(np.float32, copy=False)


# version 24
# speedup vs baseline: 1.0544x; 1.0544x over previous
import sys
from collections import defaultdict

sys.path.insert(0, "/opt/trn_rl_repo")

import numpy as np
import ml_dtypes

import concourse.bass as bass
import concourse.mybir as mybir
from concourse.bass_utils import run_bass_kernel_spmd

F32 = mybir.dt.float32
BF16 = mybir.dt.bfloat16
I32 = mybir.dt.int32
AF = mybir.ActivationFunctionType
ALU = mybir.AluOpType

NCORE = 8
BPC = 4
T = 2048
D = 1024
NCH = 8
TCH = 16
NTILE = 4
HID = 3072
NHC = 6
NPAIR = 8
S127 = float(np.float32(1.0 / 127.0))
LN_EPS = 1e-5
ATT_EPS = 1e-6
INV_D = float(np.float32(1.0) / np.float32(D))

ENGINES = ("sync", "tensor", "scalar", "vector", "gpsimd")


class Sem:
    def __init__(self, handle):
        self.h = handle
        self.n = 0


class Prog:
    def __init__(self):
        self.streams = defaultdict(list)
        self.last_wait = defaultdict(dict)

    def em(self, eng, fn):
        self.streams[eng].append(fn)

    def wait(self, eng, sem, val=None):
        if val is None:
            val = sem.n
        if val <= 0:
            return
        assert val <= sem.n, f"wait on future inc: {val} > {sem.n}"
        lw = self.last_wait[eng]
        if lw.get(sem, 0) >= val:
            return
        lw[sem] = val
        self.em(eng, lambda e, s=sem.h, v=val: e.wait_ge(s, v))

    def op(self, eng, build, inc=None, k=1):
        if inc is not None:
            inc.n += k
            self.em(eng, lambda e, b=build, s=inc.h, kk=k: b(e).then_inc(s, kk))
        else:
            self.em(eng, build)

    def dma(self, eng, sem, out_ap, in_ap):
        self.op(eng, lambda e, o=out_ap, i=in_ap: e.dma_start(out=o, in_=i),
                inc=sem, k=16)

    def flush(self, nc):
        with nc.Block() as blk:
            for eng in ENGINES:
                fns = self.streams.get(eng)
                if not fns:
                    continue
                runner = (lambda fl: lambda e: [f(e) for f in fl])(fns)
                getattr(blk, eng)(runner)


def build_nc(stage=3):
    nc = bass.Bass("TRN2", target_bir_lowering=False,
                   dynamic_dma_scratch_size=8192)

    x_d = nc.dram_tensor("x", [BPC, D, T], F32, kind="ExternalInput")
    wqt_d = nc.dram_tensor("wqt", [D, D], F32, kind="ExternalInput")
    wkt_d = nc.dram_tensor("wkt", [D, D], F32, kind="ExternalInput")
    wvt_d = nc.dram_tensor("wvt", [D, D], F32, kind="ExternalInput")
    wot_d = nc.dram_tensor("wot", [D, D], F32, kind="ExternalInput")
    bq_d = nc.dram_tensor("bq", [128, 8], F32, kind="ExternalInput")
    bo_d = nc.dram_tensor("bo", [128, 8], F32, kind="ExternalInput")
    bk_d = nc.dram_tensor("bk", [1, D], F32, kind="ExternalInput")
    bv_d = nc.dram_tensor("bv", [1, D], F32, kind="ExternalInput")
    wg_d = nc.dram_tensor("wg", [D, HID], I32, kind="ExternalInput")
    wu_d = nc.dram_tensor("wu", [D, HID], I32, kind="ExternalInput")
    wd_d = nc.dram_tensor("wd", [HID, D], I32, kind="ExternalInput")
    ones1_d = nc.dram_tensor("ones1", [1, 128], BF16, kind="ExternalInput")
    ones1f_d = nc.dram_tensor("ones1f", [1, 128], F32, kind="ExternalInput")
    onesc_d = nc.dram_tensor("onesc", [128, 1], BF16, kind="ExternalInput")
    sel_d = nc.dram_tensor("sel", [16, D], BF16, kind="ExternalInput")
    out_d = nc.dram_tensor("out", [BPC, D, T], F32, kind="ExternalOutput")
    stash = {w: nc.dram_tensor(f"stash_{w}", [NHC, 128, 4096], BF16,
                               kind="Internal") for w in ("wg", "wu", "wd")}

    from contextlib import ExitStack
    ctx = ExitStack()
    with ctx:
        SB = lambda name, shape, dt: ctx.enter_context(nc.sbuf_tensor(name, shape, dt))
        SEM = lambda name: Sem(ctx.enter_context(nc.semaphore(name)))

        x32 = SB("x32", [128, NCH * T], F32)
        y_bf = SB("y_bf", [128, NCH * T], BF16)
        arena = SB("arena", [128, 16384], BF16)
        h_arena = SB("h_arena", [128, 2048], BF16)
        wk_bf = SB("wk_bf", [128, 8192], BF16)
        wv_bf = SB("wv_bf", [128, 8192], BF16)
        wq_bf = SB("wq_bf", [128, 8192], BF16)
        wo_bf = SB("wo_bf", [128, 8192], BF16)
        wstage = SB("wstage", [128, 2090], F32)
        wstage_i = SB("wstage_i", [128, 1024], I32)
        bq_sb = SB("bq_sb", [128, 8], F32)
        bo_sb = SB("bo_sb", [128, 8], F32)
        bk_bf = SB("bk_bf", [1, D], BF16)
        bv_bf = SB("bv_bf", [1, D], BF16)
        ones1 = SB("ones1_sb", [1, 128], BF16)
        ones1f = SB("ones1f_sb", [1, 128], F32)
        onesc = SB("onesc_sb", [128, 1], BF16)
        sel = SB("sel_sb", [16, D], BF16)
        zrow = SB("zrow_sb", [1, 128], BF16)
        eps16 = SB("eps16_sb", [16, 1], F32)
        eps1 = SB("eps1_sb", [1, 1], F32)

        kbuf = [arena[:, 0:1024], arena[:, 1024:2048]]
        vbuf = [arena[:, 2048:3088], arena[:, 3088:4128]]
        xc = [arena[:, 4128:5152], arena[:, 5152:6176]]
        xt_bf = arena[:, 0:4096]
        qa_bf = arena[:, 4096:8192]
        e_sb = arena[:, 8192:8704]
        t1_sb = arena[:, 8704:9216]
        zb = [arena[:, 9216:9728], arena[:, 9728:10240]]
        kv_sb = arena[:, 10240:10760]
        den_lhsT = arena[:, 10760:10888]
        z_bf = arena[0:16, 10888:11400]
        e2_sb = arena[:, 12424:12936]
        t2_sb = arena[:, 12936:13448]
        slots = [arena[:, 4096 * i:4096 * (i + 1)] for i in range(4)]
        ln_scr = [(zb[0], zb[1]), (h_arena[:, 0:512], h_arena[:, 512:1024])]
        rb_sb = h_arena[:, 1024:1536]
        mrb_sb = h_arena[:, 1536:2048]
        r16 = arena[0:1, 11400:11912]
        mr16 = arena[0:1, 11912:12424]
        tmp32 = wstage[:, 0:512]
        ln_r = wstage[0:1, 512:1024]
        m_row = wstage[0:1, 1024:1536]
        den32 = wstage[0:16, 1536:2048]
        ln_t = wstage[0:1, 1536:2048]

        ps = [ctx.enter_context(nc.psum_tensor(f"ps{i}", [128, 512], F32))
              for i in range(8)]

        sd = SEM("sd")
        so = SEM("so")
        pe = SEM("pe")
        act = SEM("act")
        dve = SEM("dve")
        pool = SEM("pool")
        stg_sem = [SEM(f"stg{i}") for i in range(2)]
        xq_sem = [SEM(f"xq{i}") for i in range(4)]
        slot_sem = [SEM(f"slot{i}") for i in range(4)]
        st_w = SEM("st_w")
        sqw = SEM("sqw")
        svw = SEM("svw")
        scst = SEM("scst")
        stash_val = {}
        xq_state = {}

        def emit_layernorm(p, t0, src_ready_dve, want_ybf, ps_lo, ps_hi,
                           entry_pe=0, prev_apply=0, scr=None,
                           rows_on_pool=False, entry_rows=None, entry_dve=0):
            scr = scr or ln_scr
            npair = len(scr)
            hist = [entry_pe] * npair
            stat_done = entry_pe
            for m in range(NCH):
                sa, sb_ = scr[m % npair]
                srd = (src_ready_dve[m] if isinstance(src_ready_dve, list)
                       else src_ready_dve)
                p.wait("scalar", dve, srd)
                p.wait("scalar", pe, hist[m % npair])
                p.op("scalar", lambda e, mm=m, t_=t0, d_=sa: e.copy(
                    d_[:, :], x32[:, T * mm + t_:T * mm + t_ + 512]), inc=act)
                cst = act.n
                p.wait("gpsimd", dve, srd)
                p.wait("gpsimd", pe, hist[m % npair])
                p.op("gpsimd", lambda e, mm=m, t_=t0, d_=sb_: e.tensor_mul(
                    d_[:, :], x32[:, T * mm + t_:T * mm + t_ + 512],
                    x32[:, T * mm + t_:T * mm + t_ + 512]), inc=pool)
                sq = pool.n
                if m == 0:
                    p.wait("tensor", dve, entry_dve)
                p.wait("tensor", act, cst)
                p.wait("tensor", pool, sq)
                p.op("tensor", lambda e, mm=m, d_=sa: e.matmul(
                    ps[ps_lo][0:1, 0:512], onesc[:, :], d_[:, :],
                    start=(mm == 0), stop=(mm == NCH - 1)))
                p.op("tensor", lambda e, mm=m, d_=sb_: e.matmul(
                    ps[ps_hi][0:1, 0:512], onesc[:, :], d_[:, :],
                    start=(mm == 0), stop=(mm == NCH - 1)), inc=pe)
                stat_done = pe.n
                hist[m % npair] = pe.n
            p.wait("scalar", pe, stat_done)
            if entry_rows is not None:
                p.wait("scalar", entry_rows[0], entry_rows[1])
            p.op("scalar", lambda e: e.activation(
                m_row[0:1, 0:512], ps[ps_lo][0:1, 0:512], AF.Copy, scale=INV_D))
            p.op("scalar", lambda e: e.activation(
                ln_r[0:1, 0:512], ps[ps_hi][0:1, 0:512], AF.Copy, scale=INV_D),
                inc=act)
            stats_read = act.n
            if rows_on_pool:
                p.wait("gpsimd", act, stats_read)
                p.op("gpsimd", lambda e: e.tensor_mul(
                    ln_t[0:1, 0:512], m_row[0:1, 0:512], m_row[0:1, 0:512]),
                    inc=pool)
                p.wait("gpsimd", pool)
                p.op("gpsimd", lambda e: e.tensor_sub(
                    ln_r[0:1, 0:512], ln_r[0:1, 0:512], ln_t[0:1, 0:512]),
                    inc=pool)
                var_done = pool.n
                p.wait("scalar", pool, var_done)
                p.op("scalar", lambda e: e.activation(
                    ln_r[0:1, 0:512], ln_r[0:1, 0:512], AF.Ln,
                    bias=eps1[0:1, 0:1]), inc=act)
                p.wait("scalar", act)
                p.op("scalar", lambda e: e.activation(
                    ln_t[0:1, 0:512], ln_r[0:1, 0:512], AF.Exp, scale=-0.5),
                    inc=act)
                p.wait("gpsimd", act)
                p.op("gpsimd", lambda e: e.tensor_mul(
                    ln_r[0:1, 0:512], ln_t[0:1, 0:512], m_row[0:1, 0:512]),
                    inc=pool)
                rows_done = (pool, pool.n)
                p.wait("scalar", pool, pool.n)
                p.op("scalar", lambda e: e.copy(r16[0:1, :], ln_t[0:1, 0:512]))
            else:
                p.wait("vector", act, stats_read)
                p.wait("vector", dve)
                p.op("vector", lambda e: e.tensor_mul(
                    tmp32[0:1, 0:512], m_row[0:1, 0:512], m_row[0:1, 0:512]),
                    inc=dve)
                p.wait("vector", dve)
                p.op("vector", lambda e: e.scalar_tensor_tensor(
                    ln_r[0:1, 0:512], ln_r[0:1, 0:512], LN_EPS,
                    tmp32[0:1, 0:512], ALU.add, ALU.subtract), inc=dve)
                p.wait("scalar", dve)
                p.op("scalar", lambda e: e.activation(
                    ln_r[0:1, 0:512], ln_r[0:1, 0:512], AF.Ln), inc=act)
                p.wait("scalar", act)
                p.op("scalar", lambda e: e.activation(
                    tmp32[0:1, 0:512], ln_r[0:1, 0:512], AF.Exp, scale=-0.5),
                    inc=act)
                p.wait("vector", act)
                p.op("vector", lambda e: e.tensor_mul(
                    ln_r[0:1, 0:512], tmp32[0:1, 0:512], m_row[0:1, 0:512]),
                    inc=dve)
                rows_done = (dve, dve.n)
                p.wait("scalar", dve)
                p.op("scalar", lambda e: e.copy(r16[0:1, :], tmp32[0:1, 0:512]))
            p.op("scalar", lambda e: e.copy(mr16[0:1, :], ln_r[0:1, 0:512]),
                 inc=act)
            p.wait("tensor", act)
            p.op("tensor", lambda e: e.matmul(
                ps[ps_lo][:, 0:512], ones1[:, :], r16[0:1, :],
                start=True, stop=True))
            p.op("tensor", lambda e: e.matmul(
                ps[ps_hi][:, 0:512], ones1[:, :], mr16[0:1, :],
                start=True, stop=True), inc=pe)
            bc = pe.n
            p.wait("scalar", pe, bc)
            p.wait("scalar", dve, prev_apply)
            p.op("scalar", lambda e: e.copy(rb_sb[:, :], ps[ps_lo][:, 0:512]))
            p.op("scalar", lambda e: e.copy(mrb_sb[:, :], ps[ps_hi][:, 0:512]),
                 inc=act)
            bcast_evac = act.n
            p.wait("vector", act, bcast_evac)
            for m in range(NCH):
                p.wait("vector", dve)
                p.op("vector", lambda e, mm=m, t_=t0: e.tensor_mul(
                    tmp32[:, :], x32[:, T * mm + t_:T * mm + t_ + 512],
                    rb_sb[:, :]), inc=dve)
                p.wait("vector", dve)
                p.op("vector", lambda e, mm=m, t_=t0: e.tensor_sub(
                    x32[:, T * mm + t_:T * mm + t_ + 512], tmp32[:, :],
                    mrb_sb[:, :]), inc=dve)
                if want_ybf:
                    p.wait("scalar", dve)
                    p.op("scalar", lambda e, mm=m, t_=t0: e.copy(
                        y_bf[:, T * mm + t_:T * mm + t_ + 512],
                        x32[:, T * mm + t_:T * mm + t_ + 512]), inc=act)
            return stats_read, dve.n, bcast_evac, rows_done

        p = Prog()
        p.dma("sync", scst, bq_sb[:], bq_d[:])
        p.dma("sync", scst, bo_sb[:], bo_d[:])
        p.dma("sync", scst, ones1[:], ones1_d[:])
        p.dma("sync", scst, ones1f[:], ones1f_d[:])
        p.dma("sync", scst, onesc[:], onesc_d[:])
        p.dma("sync", scst, sel[:], sel_d[:])
        p.dma("sync", sd, wstage[0:1, 0:D], bk_d[:])
        for _eng in ("tensor", "scalar", "vector"):
            p.wait(_eng, scst)
        p.wait("scalar", sd)
        p.op("scalar", lambda e: e.copy(bk_bf[:], wstage[0:1, 0:D]), inc=act)
        p.wait("sync", act)
        p.dma("sync", sd, wstage[0:1, 0:D], bv_d[:])
        p.wait("scalar", sd)
        p.op("scalar", lambda e: e.copy(bv_bf[:], wstage[0:1, 0:D]), inc=act)
        p.op("vector", lambda e: e.memset(den_lhsT[:], 0.0))
        p.op("vector", lambda e: e.memset(eps16[:], ATT_EPS))
        p.op("vector", lambda e: e.memset(eps1[:], LN_EPS))
        p.op("vector", lambda e: e.memset(zrow[:], 0.0), inc=dve)
        init_dve = dve.n
        wcast = {}

        def prefetch_x(b):
            st = []
            for quar in range(4):
                qs = xq_sem[quar]
                p.dma("sync", qs,
                      x32[:].rearrange("p (k t) -> p k t", k=NCH)
                      [:, :, 512 * quar:512 * (quar + 1)],
                      x_d[b].rearrange("(k p) t -> p k t", p=128)
                      [:, :, 512 * quar:512 * (quar + 1)])
                st.append((qs, qs.n))
            xq_state[b] = st

        bias_ready = act.n
        prefetch_x(0)
        wcast["wk"] = []
        wcast["wv"] = []
        p.wait("scalar", act, bias_ready)
        p.dma("scalar", sqw, wstage[:, 0:D], wkt_d[0:128, :])
        p.dma("scalar", svw, wstage[:, 1045:1045 + D], wvt_d[0:128, :])
        for k in range(NCH):
            p.wait("scalar", sqw, 16 * (k + 1))
            p.op("scalar", lambda e, kk=k:
                 e.copy(wk_bf[:, D * kk:D * (kk + 1)], wstage[:, 0:D]),
                 inc=act)
            wcast["wk"].append(act.n)
            if k + 1 < NCH:
                p.wait("scalar", act, wcast["wk"][k])
                p.dma("scalar", sqw, wstage[:, 0:D],
                      wkt_d[128 * (k + 1):128 * (k + 2), :])
            p.wait("scalar", svw, 16 * (k + 1))
            p.op("scalar", lambda e, kk=k:
                 e.copy(wv_bf[:, D * kk:D * (kk + 1)],
                        wstage[:, 1045:1045 + D]), inc=act)
            wcast["wv"].append(act.n)
            if k + 1 < NCH:
                p.wait("scalar", act, wcast["wv"][k])
                p.dma("scalar", svw, wstage[:, 1045:1045 + D],
                      wvt_d[128 * (k + 1):128 * (k + 2), :])
        kv_w_ready = act.n

        wcast["wq"] = []
        wcast["wo"] = []
        wq_dma = []
        wo_dma = []
        p.wait("sync", act, kv_w_ready)
        p.dma("sync", sd, wstage[:, 1045:1045 + D], wqt_d[0:128, :])
        wq_dma.append(sd.n)
        p.dma("sync", sqw, wstage[:, 0:D], wot_d[0:128, :])
        wo_dma.append(sqw.n)
        defer_cast = {2: [("wq", 0)], 3: [("wq", 1)], 4: [("wq", 2)],
                      5: [("wq", 3)], 6: [("wq", 4)], 7: [("wq", 5)],
                      8: [("wq", 6)], 9: [("wq", 7)],
                      10: [("wo", 0), ("wo", 1)], 11: [("wo", 2), ("wo", 3)],
                      12: [("wo", 4)], 13: [("wo", 5)], 14: [("wo", 6)],
                      15: [("wo", 7)]}

        def emit_deferred_cast(wname, j):
            qsem, slot, wsb, wdram, dvals = (
                (sd, 1045, wq_bf, wqt_d, wq_dma) if wname == "wq"
                else (sqw, 0, wo_bf, wot_d, wo_dma))
            p.wait("scalar", qsem, dvals[j])
            p.op("scalar", lambda e, jj=j, w=wsb, so_=slot:
                 e.copy(w[:, D * jj:D * (jj + 1)], wstage[:, so_:so_ + D]),
                 inc=act)
            wcast[wname].append(act.n)
            if j + 1 < NCH:
                p.wait("sync", act, wcast[wname][j])
                p.dma("sync", qsem, wstage[:, slot:slot + D],
                      wdram[128 * (j + 1):128 * (j + 2), :])
                dvals.append(qsem.n)

        prev_rows = None
        ln2_rows_pool = 0
        for b in range(BPC):
            xq_loaded = xq_state[b]
            if b > 0:
                p.wait("tensor", act, prev_guard["p3_act"])
                p.wait("tensor", dve, prev_guard["ffn_done"])
                p.wait("tensor", act, prev_guard["ln2_evac"])
                p.wait("vector", pe)
                for _ss in slot_sem:
                    p.wait("scalar", _ss)

            kevac_done = [0] * TCH
            phi_free = [0, 0]
            vevac_done = [0] * TCH
            kv_done = [0] * TCH
            xc_done = [0] * TCH
            xcw = [0] * TCH
            psK_free = [0, 0]
            psV_free = [0, 0]

            def emit_xc(c):
                qs, qv = xq_loaded[c // 4]
                p.wait("vector", qs, qv)
                if c >= 2:
                    p.wait("vector", pe, xcw[c - 2])
                p.op("vector", lambda e, cc=c: e.tensor_copy(
                    xc[cc % 2][:].rearrange("p (k t) -> p k t", k=NCH),
                    x32[:].rearrange("p (k t) -> p k t", k=NCH)
                    [:, :, 128 * cc:128 * (cc + 1)]), inc=dve)
                xc_done[c] = dve.n

            def emit_vgroup(c, h):
                p.wait("tensor", act, max(psV_free[h], bias_ready))
                p.wait("tensor", dve, xc_done[c])
                for k in range(NCH):
                    p.wait("tensor", act, wcast["wv"][k])
                    p.op("tensor", lambda e, cc=c, hh=h, kk=k:
                         e.matmul(ps[2 + hh][:, 0:512],
                                  xc[cc % 2][:, 128 * kk:128 * (kk + 1)],
                                  wv_bf[:, D * kk + 512 * hh:D * kk + 512 * (hh + 1)],
                                  start=(kk == 0), stop=False))
                p.op("tensor", lambda e, hh=h:
                     e.matmul(ps[2 + hh][:, 0:512], ones1[:, :],
                              bv_bf[0:1, 512 * hh:512 * (hh + 1)],
                              start=False, stop=True), inc=pe)
                return pe.n

            def emit_vevac(c, h, vproj_h):
                p.wait("scalar", pe, vproj_h)
                p.wait("scalar", dve, vmemset_done)
                if c >= 2:
                    p.wait("scalar", pe, kv_done[c - 2])
                voff = 2048 + 1040 * (c % 2) + 520 * h
                p.op("scalar", lambda e, hh=h, vo=voff:
                     e.copy(bass.AP(arena, vo, [[16384, 128], [65, 8], [1, 64]]),
                            ps[2 + hh][:, 0:512].rearrange(
                                "p (i s) -> p i s", s=64)), inc=act)
                psV_free[h] = act.n

            def emit_kv(c):
                p.wait("tensor", dve, kevac_done[c])
                p.wait("tensor", act, vevac_done[c])
                for pr in range(NPAIR):
                    bank = 4 + pr // 2
                    col0 = 130 * (pr % 2)
                    p.op("tensor", lambda e, cc=c, b_=bank, c0=col0, pp=pr:
                         e.matmul(ps[b_][:, c0:c0 + 130],
                                  kbuf[cc % 2][:, 128 * pp:128 * (pp + 1)],
                                  vbuf[cc % 2][:, 130 * pp:130 * (pp + 1)],
                                  start=False,
                                  stop=(cc == TCH - 1),
                                  skip_group_check=True),
                         inc=(pe if pr == NPAIR - 1 else None))
                kv_done[c] = pe.n

            def emit_proj_and_evacs(c, prev, warmup=False):
                kproj_h = [0, 0]
                if warmup:
                    qs0, qv0 = xq_loaded[0]
                    p.wait("tensor", qs0, qv0)
                for h in range(2):
                    p.wait("tensor", act, max(psK_free[h], bias_ready))
                    p.wait("tensor", dve, xc_done[c])
                    for k in range(NCH):
                        if warmup and h == 0:
                            for _d in range(4):
                                p.op("tensor", lambda e: e.matmul(
                                    ps[2][0:8, 0:512], bq_sb[:, :],
                                    x32[:, 0:512], start=True, stop=True))
                        p.wait("tensor", act, wcast["wk"][k])
                        p.op("tensor", lambda e, cc=c, hh=h, kk=k:
                             e.matmul(ps[hh][:, 0:512],
                                      xc[cc % 2][:, 128 * kk:128 * (kk + 1)],
                                      wk_bf[:, D * kk + 512 * hh:D * kk + 512 * (hh + 1)],
                                      start=(kk == 0), stop=False))
                    p.op("tensor", lambda e, hh=h:
                         e.matmul(ps[hh][:, 0:512], ones1[:, :],
                                  bk_bf[0:1, 512 * hh:512 * (hh + 1)],
                                  start=False, stop=True), inc=pe)
                    kproj_h[h] = pe.n
                kproj = pe.n
                v0 = emit_vgroup(c, 0)
                emit_vevac(c, 0, v0)
                if prev is not None:
                    emit_kv(prev)
                v1 = emit_vgroup(c, 1)
                xcw[c] = pe.n
                emit_vevac(c, 1, v1)
                vevac_done[c] = act.n
                for h in range(2):
                    eb, tb = (e_sb, t1_sb) if h == 0 else (e2_sb, t2_sb)
                    p.wait("scalar", pe, kproj_h[h])
                    p.wait("scalar", dve, phi_free[h])
                    p.op("scalar", lambda e, hh=h, b1=eb:
                         e.activation(b1[:, :], ps[hh][:, 0:512], AF.Exp))
                    p.op("scalar", lambda e, hh=h, b2=tb:
                         e.activation(b2[:, :], ps[hh][:, 0:512], AF.Relu),
                         inc=act)
                    psK_free[h] = act.n
                    p.wait("vector", act)
                    if c >= 2:
                        p.wait("vector", pe, kv_done[c - 2])
                    p.op("vector", lambda e, cc=c, hh=h, b1=eb, b2=tb:
                         e.scalar_tensor_tensor(
                             kbuf[cc % 2][:, 512 * hh:512 * (hh + 1)],
                             b1[:, :], 1.0, b2[:, :], ALU.min, ALU.add),
                         inc=dve)
                    phi_free[h] = dve.n
                kevac_done[c] = dve.n

            for vb in range(2):
                base = 2048 + 1040 * vb + 64
                p.op("vector", lambda e, b_=base: e.memset(
                    bass.AP(arena, b_, [[16384, 128], [65, 16], [1, 1]]), 1.0),
                    inc=(dve if vb == 1 else None))
            vmemset_done = dve.n

            p.wait("tensor", act, bias_ready)
            p.wait("tensor", dve, init_dve)
            for t_ in range(4):
                p.op("tensor", lambda e, b_=4 + t_:
                     e.matmul(ps[b_][:, 0:260], zrow[:, :], bv_bf[0:1, 0:260],
                              start=True, stop=False, skip_group_check=True))

            emit_xc(0)
            prev = None
            for c in range(TCH):
                if c + 1 < TCH:
                    emit_xc(c + 1)
                emit_proj_and_evacs(c, prev, warmup=(b == 0 and c == 0))
                if b == 0:
                    for wname_, j_ in defer_cast.get(c, []):
                        emit_deferred_cast(wname_, j_)
                prev = c
            emit_kv(TCH - 1)

            p.wait("scalar", pe, kv_done[TCH - 1])
            for pr in range(NPAIR):
                bank = 4 + pr // 2
                c0 = 130 * (pr % 2)
                p.op("scalar", lambda e, b_=bank, cc=c0, pp=pr:
                     e.copy(kv_sb[0:64, 65 * pp:65 * (pp + 1)],
                            ps[b_][0:64, cc:cc + 65]))
                p.op("scalar", lambda e, b_=bank, cc=c0, pp=pr:
                     e.copy(kv_sb[64:128, 65 * pp:65 * (pp + 1)],
                            ps[b_][64:128, cc + 65:cc + 130]), inc=act)
            p.wait("scalar", act)
            for hh in range(16):
                r0 = 64 * (hh % 2)
                col = 16 * (hh // 2) + hh
                p.op("scalar", lambda e, r=r0, cl=col, pp=hh // 2:
                     e.activation(den_lhsT[r:r + 64, cl:cl + 1],
                                  kv_sb[r:r + 64, 65 * pp + 64:65 * pp + 65],
                                  AF.Copy, bias=ATT_EPS), inc=act)
            kv_ready = act.n

            q_free = [psK_free[0], psK_free[1], psV_free[0], psV_free[1]]
            q_free_d = [0, 0, 0, 0]
            zb_free = [0, 0]
            q_stt = [0] * NCH
            a_mult = [0] * NCH
            stats_free = kv_ready
            bcast_free = 0
            ln1_apply = 0
            q_mm_last = kv_done[TCH - 1]
            xt_ready_t = [0] * (NTILE + 1)

            def emit_xt(tile):
                t0_ = 512 * tile
                p.wait("vector", pe, q_mm_last)
                qs_, qv_ = xq_loaded[tile]
                p.wait("vector", qs_, qv_)
                p.op("vector", lambda e, t_=t0_: e.tensor_copy(
                    xt_bf[:].rearrange("p (k t) -> p k t", k=NCH),
                    x32[:].rearrange("p (k t) -> p k t", k=NCH)
                    [:, :, t_:t_ + 512]), inc=dve)
                xt_ready_t[tile] = dve.n

            emit_xt(0)
            for tile in range(NTILE):
                t0 = 512 * tile
                xt_ready = xt_ready_t[tile]
                for m in range(NCH):
                    bank = m % 4
                    p.wait("tensor", act, q_free[bank])
                    p.wait("tensor", dve, max(q_free_d[bank], xt_ready))
                    for k in range(NCH):
                        p.wait("tensor", act, wcast["wq"][k])
                        p.op("tensor", lambda e, mm=m, kk=k, b_=bank:
                             e.matmul(ps[b_][:, 0:512],
                                      wq_bf[:, D * kk + 128 * mm:D * kk + 128 * (mm + 1)],
                                      xt_bf[:, 512 * kk:512 * (kk + 1)],
                                      start=(kk == 0), stop=(kk == NCH - 1)),
                             inc=(pe if k == NCH - 1 else None))
                    qmm = pe.n
                    eb, tb = (e_sb, t1_sb) if m % 2 == 0 else (e2_sb, t2_sb)
                    p.wait("scalar", pe, qmm)
                    p.wait("scalar", dve, phi_free[m % 2])
                    p.op("scalar", lambda e, b_=bank, mm=m, b1=eb:
                         e.activation(b1[:, :], ps[b_][:, 0:512], AF.Exp,
                                      bias=bq_sb[:, mm:mm + 1]))
                    p.op("scalar", lambda e, b_=bank, mm=m, b2=tb:
                         e.activation(b2[:, :], ps[b_][:, 0:512], AF.Relu,
                                      bias=bq_sb[:, mm:mm + 1]), inc=act)
                    q_free[bank] = act.n
                    p.wait("vector", act)
                    p.op("vector", lambda e, mm=m, b1=eb, b2=tb:
                         e.scalar_tensor_tensor(qa_bf[:, 512 * mm:512 * (mm + 1)],
                                                b1[:, :], 1.0, b2[:, :],
                                                ALU.min, ALU.add), inc=dve)
                    q_stt[m] = dve.n
                    phi_free[m % 2] = dve.n
                q_all = dve.n
                q_mm_last = pe.n
                if tile + 1 < NTILE:
                    emit_xt(tile + 1)
                p.wait("tensor", dve, q_free_d[2])
                p.wait("tensor", act, kv_ready)
                p.wait("tensor", act, q_free[2])
                for c in range(NCH):
                    p.wait("tensor", dve, q_stt[c])
                    p.op("tensor", lambda e, cc=c:
                         e.matmul(ps[2][0:16, 0:512],
                                  den_lhsT[:, 16 * cc:16 * (cc + 1)],
                                  qa_bf[:, 512 * cc:512 * (cc + 1)],
                                  start=(cc == 0), stop=(cc == NCH - 1)),
                         inc=(pe if c == NCH - 1 else None))
                p.wait("scalar", pe, pe.n)
                p.wait("scalar", pool, ln2_rows_pool)
                p.op("scalar", lambda e: e.activation(
                    den32[:, :], ps[2][0:16, 0:512], AF.Ln,
                    bias=eps16[:, 0:1]), inc=act)
                q_free[2] = act.n
                p.wait("scalar", act)
                p.op("scalar", lambda e: e.activation(
                    z_bf[:, :], den32[:, :], AF.Exp, scale=-1.0), inc=act)
                z_ready = act.n
                for c in range(NCH):
                    p.wait("tensor", act, z_ready)
                    p.wait("tensor", act)
                    p.op("tensor", lambda e, cc=c:
                         e.matmul(ps[3][:, 0:512], sel[:, 128 * cc:128 * (cc + 1)],
                                  z_bf[:, :], start=True, stop=True), inc=pe)
                    zmm = pe.n
                    p.wait("scalar", pe, zmm)
                    p.wait("scalar", dve, zb_free[c % 2])
                    p.op("scalar", lambda e, cc=c:
                         e.copy(zb[cc % 2][:, :], ps[3][:, 0:512]), inc=act)
                    zev = act.n
                    bank = 4 + c % 2
                    p.wait("tensor", dve, zb_free[c % 2])
                    p.op("tensor", lambda e, cc=c, b_=bank:
                         e.matmul(ps[b_][0:64, 0:512],
                                  kv_sb[0:64, 65 * cc:65 * cc + 64],
                                  qa_bf[0:64, 512 * cc:512 * (cc + 1)],
                                  start=True, stop=True))
                    p.op("tensor", lambda e, cc=c, b_=bank:
                         e.matmul(ps[b_][64:128, 0:512],
                                  kv_sb[64:128, 65 * cc:65 * cc + 64],
                                  qa_bf[64:128, 512 * cc:512 * (cc + 1)],
                                  start=True, stop=True), inc=pe)
                    amm = pe.n
                    p.wait("vector", pe, amm)
                    p.wait("vector", act, zev)
                    p.op("vector", lambda e, cc=c, b_=bank:
                         e.tensor_mul(qa_bf[:, 512 * cc:512 * (cc + 1)],
                                      ps[b_][:, 0:512], zb[cc % 2][:, :]),
                         inc=dve)
                    zb_free[c % 2] = dve.n
                    a_mult[c] = dve.n
                a_all = dve.n
                q_free[3] = act.n
                o_free = [0, 0]
                for m in range(NCH):
                    bank = 6 + m % 2
                    p.wait("tensor", dve, o_free[m % 2])
                    p.wait("tensor", act, max(stats_free, bcast_free))
                    for k in range(NCH):
                        p.wait("tensor", act, wcast["wo"][k])
                        p.wait("tensor", dve, a_mult[k])
                        p.op("tensor", lambda e, mm=m, kk=k, b_=bank:
                             e.matmul(ps[b_][:, 0:512],
                                      wo_bf[:, D * kk + 128 * mm:D * kk + 128 * (mm + 1)],
                                      qa_bf[:, 512 * kk:512 * (kk + 1)],
                                      start=(kk == 0), stop=(kk == NCH - 1)),
                             inc=(pe if k == NCH - 1 else None))
                    omm = pe.n
                    p.wait("vector", pe, omm)
                    p.op("vector", lambda e, mm=m, b_=bank, t_=t0:
                         e.scalar_tensor_tensor(
                             x32[:, T * mm + t_:T * mm + t_ + 512],
                             ps[b_][:, 0:512], bo_sb[:, mm:mm + 1],
                             x32[:, T * mm + t_:T * mm + t_ + 512],
                             ALU.add, ALU.add), inc=dve)
                    o_free[m % 2] = dve.n
                o_done = dve.n
                stats_free, ln1_apply, bcast_free, prev_rows = emit_layernorm(
                    p, t0, o_done, True, 6, 7, prev_apply=ln1_apply,
                    entry_rows=prev_rows)

            if stage == 1:
                p.flush(nc)
                p = Prog()
                p.wait("sync", dve)
                p.wait("sync", act)
                p.dma("sync", so,
                      out_d[b].rearrange("(k p) t -> p k t", p=128),
                      x32[:].rearrange("p (k t) -> p k t", k=NCH))
                p.wait("sync", so)
                p.flush(nc)
                continue

            p3_pe = pe.n
            p3_act = act.n
            p3_dve = dve.n
            for _eng in ("gpsimd", "vector", "scalar"):
                p.wait(_eng, pe, p3_pe)
                p.wait(_eng, act, p3_act)
                p.wait(_eng, dve, p3_dve)
            stg_free = [None, None]
            stg_of = [0, 512]
            conv_eng = {"wg": ("gpsimd", pool), "wg2": ("vector", dve),
                        "wu": ("vector", dve), "wd": ("scalar", act)}
            load_q = {"wg": "sync", "wu": "sync", "wd": "scalar"}
            slot_map = [(0, 1, 2), (0, 1, 3), (0, 1, 2), (0, 1, 3),
                        (0, 1, 2), (0, 1, 3)]
            slot_war = [p3_pe] * 4
            slot_ready_k = [dict() for _ in range(4)]
            stash_rd = {}
            slot_stash_rd = {}

            def emit_ffn_loads(hc):
                sgl, sul, sdl = slot_map[hc]
                for wname, wdram, slot_i in (("wg", wg_d, sgl),
                                             ("wu", wu_d, sul),
                                             ("wd", wd_d, sdl)):
                    dest = slots[slot_i]
                    if b > 0:
                        ssem = slot_sem[slot_i]
                        ldq = "scalar" if wname == "wu" else "sync"
                        p.wait(ldq, *stash_val[(wname, hc)])
                        p.wait(ldq, pe, slot_war[slot_i])
                        p.dma(ldq, ssem, dest[:, :], stash[wname][hc])
                        rdy = (ssem, ssem.n)
                        for kk in range(NCH):
                            slot_ready_k[slot_i][kk] = rdy
                        continue
                    qeng = load_q[wname]
                    for s in range(8):
                        ceng, csem = conv_eng[
                            "wg2" if (wname == "wg" and s % 2) else wname]
                        sub = 0 if wname != "wd" else 1
                        qsem = stg_sem[sub]
                        col = stg_of[sub]
                        if stg_free[sub] is not None:
                            p.wait(qeng, *stg_free[sub])
                        if wname in ("wg", "wu"):
                            srcw = wdram[128 * s:128 * (s + 1),
                                         512 * hc:512 * (hc + 1)]
                            dcol = 512 * s
                            kk = s
                        else:
                            r0 = 512 * hc + 128 * (s // 2)
                            c0 = 512 * (s % 2)
                            srcw = wdram[r0:r0 + 128, c0:c0 + 512]
                            dcol = 1024 * (s // 2) + 512 * (s % 2)
                            kk = s // 2
                        p.op(qeng, lambda e, cl=col, sr=srcw:
                             e.dma_start(out=wstage_i[:, cl:cl + 512], in_=sr),
                             inc=qsem, k=16)
                        p.wait(ceng, qsem, qsem.n)
                        p.wait(ceng, pe, slot_war[slot_i])
                        if (wname, hc) in stash_rd:
                            p.wait(ceng, *stash_rd.pop((wname, hc), (st_w, 0)))
                        prev_rd = slot_stash_rd.get(slot_i)
                        if prev_rd is not None:
                            p.wait(ceng, *prev_rd)
                        if ceng == "scalar":
                            p.op(ceng, lambda e, cl=col, dc=dcol, d_=dest:
                                 e.copy(d_[:, dc:dc + 512],
                                        wstage_i[:, cl:cl + 512]), inc=csem)
                        else:
                            p.op(ceng, lambda e, cl=col, dc=dcol, d_=dest:
                                 e.tensor_copy(d_[:, dc:dc + 512],
                                               wstage_i[:, cl:cl + 512]),
                                 inc=csem)
                        stg_free[sub] = (csem, csem.n)
                        slot_ready_k[slot_i][kk] = (csem, csem.n)
                        last_conv = csem.n
                    ssem = slot_sem[slot_i]
                    p.wait("sync", csem, last_conv)
                    p.dma("sync", ssem, stash[wname][hc], dest[:, :])
                    stash_val[(wname, hc)] = (ssem, ssem.n)
                    slot_stash_rd[slot_i] = (ssem, ssem.n)

            emit_ffn_loads(0)
            gate_free = [p3_act, p3_act]
            up_free = [p3_dve, p3_dve]
            down_free = [p3_dve] * 4
            h_war = 0
            x32_guard = [[p3_dve] * NCH for _ in range(NTILE)]

            for hc in range(NHC):
                sgl, sul, sdl = slot_map[hc]
                for tile in range(NTILE):
                    t0 = 512 * tile
                    silu_done = [0] * 4
                    h_stt = [0] * 4
                    for g in range(4):
                        bank = g % 2
                        p.wait("tensor", act, gate_free[bank])
                        for k in range(NCH):
                            p.wait("tensor", *slot_ready_k[sgl][k])
                            p.op("tensor", lambda e, gg=g, kk=k, b_=bank, sl_=sgl, t_=t0:
                                 e.matmul(ps[b_][:, 0:512],
                                          slots[sl_][:, 512 * kk + 128 * gg:512 * kk + 128 * (gg + 1)],
                                          y_bf[:, T * kk + t_:T * kk + t_ + 512],
                                          start=(kk == 0), stop=(kk == NCH - 1)),
                                 inc=(pe if k == NCH - 1 else None))
                        gmm = pe.n
                        slot_war[sgl] = pe.n
                        p.wait("scalar", pe, gmm)
                        p.wait("scalar", pe, h_war)
                        p.op("scalar", lambda e, gg=g, b_=bank:
                             e.activation(h_arena[:, 512 * gg:512 * (gg + 1)],
                                          ps[b_][:, 0:512], AF.Silu,
                                          scale=S127), inc=act)
                        silu_done[g] = act.n
                        gate_free[bank] = act.n
                    for g in range(4):
                        bank = 2 + g % 2
                        p.wait("tensor", dve, up_free[g % 2])
                        for k in range(NCH):
                            p.wait("tensor", *slot_ready_k[sul][k])
                            p.op("tensor", lambda e, gg=g, kk=k, b_=bank, sl_=sul, t_=t0:
                                 e.matmul(ps[b_][:, 0:512],
                                          slots[sl_][:, 512 * kk + 128 * gg:512 * kk + 128 * (gg + 1)],
                                          y_bf[:, T * kk + t_:T * kk + t_ + 512],
                                          start=(kk == 0), stop=(kk == NCH - 1)),
                                 inc=(pe if k == NCH - 1 else None))
                        umm = pe.n
                        slot_war[sul] = pe.n
                        p.wait("vector", pe, umm)
                        p.wait("vector", act, silu_done[g])
                        p.wait("vector", pe, h_war)
                        p.op("vector", lambda e, gg=g, b_=bank:
                             e.scalar_tensor_tensor(
                                 h_arena[:, 512 * gg:512 * (gg + 1)],
                                 ps[b_][:, 0:512], S127,
                                 h_arena[:, 512 * gg:512 * (gg + 1)],
                                 ALU.mult, ALU.mult), inc=dve)
                        up_free[g % 2] = dve.n
                        h_stt[g] = dve.n
                    for m in range(NCH):
                        bank = 4 + m % 4
                        p.wait("tensor", dve, down_free[m % 4])
                        for s in range(4):
                            p.wait("tensor", *slot_ready_k[sdl][s])
                            p.wait("tensor", dve, h_stt[s])
                            p.op("tensor", lambda e, mm=m, ss=s, b_=bank, sl_=sdl:
                                 e.matmul(ps[b_][:, 0:512],
                                          slots[sl_][:, 1024 * ss + 128 * mm:1024 * ss + 128 * (mm + 1)],
                                          h_arena[:, 512 * ss:512 * (ss + 1)],
                                          start=(ss == 0), stop=(ss == 3)),
                                 inc=(pe if s == 3 else None))
                        dmm = pe.n
                        if m == NCH - 1:
                            h_war = pe.n
                        slot_war[sdl] = pe.n
                        p.wait("vector", pe, dmm)
                        p.wait("vector", dve, x32_guard[tile][m])
                        p.op("vector", lambda e, mm=m, b_=bank, t_=t0:
                             e.scalar_tensor_tensor(
                                 x32[:, T * mm + t_:T * mm + t_ + 512],
                                 ps[b_][:, 0:512], S127,
                                 x32[:, T * mm + t_:T * mm + t_ + 512],
                                 ALU.mult, ALU.add), inc=dve)
                        down_free[m % 4] = dve.n
                        x32_guard[tile][m] = dve.n
                if hc + 1 < NHC:
                    emit_ffn_loads(hc + 1)
            ffn_done = dve.n
            prev_guard = {"p3_act": act.n, "ffn_done": ffn_done, "ln2_evac": 0}

            for _ss in slot_sem:
                p.wait("scalar", _ss)
                p.wait("gpsimd", _ss)
            p.wait("scalar", pe)
            p.wait("gpsimd", pe)
            prev_apply = ffn_done
            prev_bc = pe.n
            ln2_apply = 0
            ln2_scr = [(zb[0], zb[1]),
                       (h_arena[:, 0:512], h_arena[:, 512:1024]),
                       (slots[0][:, 0:512], slots[0][:, 512:1024]),
                       (slots[1][:, 0:512], slots[1][:, 512:1024])]
            next_x = []
            for tile in range(NTILE):
                t0 = 512 * tile
                _, ap_done, _bev, prev_rows = emit_layernorm(
                    p, t0, x32_guard[tile], False, 6, 7, entry_pe=prev_bc,
                    prev_apply=ln2_apply, scr=ln2_scr,
                    rows_on_pool=True, entry_rows=prev_rows,
                    entry_dve=(ffn_done if tile == 0 else 0))
                ln2_rows_pool = prev_rows[1]
                ln2_apply = ap_done
                prev_bc = pe.n
                prev_apply = ap_done
                prev_guard["ln2_evac"] = _bev
                p.wait("sync", dve, ap_done)
                p.dma("sync", so,
                      out_d[b].rearrange("(k p) t -> p k t", p=128)
                      [:, :, t0:t0 + 512],
                      x32[:].rearrange("p (k t) -> p k t", k=NCH)
                      [:, :, t0:t0 + 512])
                if b + 1 < BPC:
                    p.wait("sync", so, so.n)
                    qs = xq_sem[tile]
                    p.dma("sync", qs,
                          x32[:].rearrange("p (k t) -> p k t", k=NCH)
                          [:, :, t0:t0 + 512],
                          x_d[b + 1].rearrange("(k p) t -> p k t", p=128)
                          [:, :, t0:t0 + 512])
                    next_x.append((qs, qs.n))
            if b + 1 < BPC:
                xq_state[b + 1] = next_x
            p.wait("sync", so)
        p.flush(nc)

    return nc


def _host_inputs(L0_attn_x, L0_attn_Wq, L0_attn_bq, L0_attn_Wk, L0_attn_bk,
                 L0_attn_Wv, L0_attn_bv, L0_attn_Wo, L0_attn_bo,
                 L0_ffn_w_gate, L0_ffn_w_up, L0_ffn_w_down):
    bf = ml_dtypes.bfloat16
    x = np.asarray(L0_attn_x, dtype=np.float32)
    x_fm = np.ascontiguousarray(
        x.reshape(NCORE, BPC, T, D).transpose(0, 1, 3, 2))

    wqt = np.ascontiguousarray(np.asarray(L0_attn_Wq, np.float32).T)
    wkt = np.ascontiguousarray(np.asarray(L0_attn_Wk, np.float32).T)
    wot = np.ascontiguousarray(np.asarray(L0_attn_Wo, np.float32).T)
    wvt = np.ascontiguousarray(np.asarray(L0_attn_Wv, np.float32).T)
    bv_pad = np.ascontiguousarray(np.asarray(L0_attn_bv, np.float32)[None, :])

    bq_pc = np.ascontiguousarray(np.asarray(L0_attn_bq, np.float32).reshape(8, 128).T)
    bo_pc = np.ascontiguousarray(np.asarray(L0_attn_bo, np.float32).reshape(8, 128).T)
    bk_row = np.ascontiguousarray(np.asarray(L0_attn_bk, np.float32)[None, :])

    wg = np.ascontiguousarray(np.asarray(L0_ffn_w_gate, np.int32).T)
    wu = np.ascontiguousarray(np.asarray(L0_ffn_w_up, np.int32).T)
    wd = np.ascontiguousarray(np.asarray(L0_ffn_w_down, np.int32).T)

    ones1 = np.ones((1, 128), bf)
    ones1f = np.ones((1, 128), np.float32)
    onesc = np.ones((128, 1), bf)
    sel = np.zeros((16, D), bf)
    for c in range(NCH):
        sel[2 * c, 128 * c:128 * c + 64] = 1
        sel[2 * c + 1, 128 * c + 64:128 * (c + 1)] = 1

    shared = dict(wqt=wqt, wkt=wkt, wvt=wvt, wot=wot, bq=bq_pc, bo=bo_pc,
                  bk=bk_row, bv=bv_pad, wg=wg, wu=wu, wd=wd, ones1=ones1,
                  ones1f=ones1f, onesc=onesc, sel=sel)
    return [dict(shared, x=np.ascontiguousarray(x_fm[c])) for c in range(NCORE)]


def kernel(**inputs):
    in_maps = _host_inputs(**inputs)
    nc = build_nc(stage=3)
    res = run_bass_kernel_spmd(nc, in_maps, core_ids=list(range(NCORE)))
    out_fm = np.stack([np.asarray(r["out"]) for r in res.results])
    out = out_fm.transpose(0, 1, 3, 2).reshape(NCORE * BPC, T, D)
    return np.ascontiguousarray(out).astype(np.float32, copy=False)
